# version 05
# baseline (speedup 1.0000x reference)
import numpy as np
import ml_dtypes
from contextlib import ExitStack
import concourse.bass as bass
import concourse.mybir as mybir
from concourse.bass_utils import run_bass_kernel_spmd

F32 = mybir.dt.float32
BF16 = mybir.dt.bfloat16
F16 = mybir.dt.float16
I32 = mybir.dt.int32
I16 = mybir.dt.int16
AF = mybir.ActivationFunctionType
ALU = mybir.AluOpType
AX = mybir.AxisListType
bfdt = ml_dtypes.bfloat16

DM = 4096
DIN = 8192
NH = 128
HD = 64
NG = 8
DS = 128
CONVK = 4
CONVD = DIN + 2 * NG * DS
DPROJ = 2 * DIN + 2 * NG * DS + NH
INTER = 11008
GSZ = 128
EPS = 1e-5
NCORES = 8

COMPUTE_AGENTS = ("pe", "dve", "act", "pool", "sp")
DMA_POOL = 8
DMA_QUEUES = ("dsp", "dact", "dpool")
DMA_AGENTS = tuple(f"{q}{i}" for q in DMA_QUEUES for i in range(DMA_POOL)) + ("dcoll",)
DMA_OF = {"sp": "dsp", "act": "dact", "pool": "dpool"}


class P:

    def __init__(self, nc):
        self.nc = nc
        self.q = {a: [] for a in COMPUTE_AGENTS}
        self.cnt = {a: 0 for a in COMPUTE_AGENTS + DMA_AGENTS}
        self.sat = {a: {t: 0 for t in COMPUTE_AGENTS + DMA_AGENTS}
                    for a in COMPUTE_AGENTS}
        self.res = {}
        self.sems = {}
        self.dmarr = {q: 0 for q in DMA_QUEUES}
        self.nops = 0

    def _resolve(self, sem, cnt, snap, tag):
        if cnt is not None:
            return cnt
        cur = self.cnt[sem]
        if cur > snap:
            return cur
        raise RuntimeError(
            f"{tag}: pending dep via {sem}: no inc after access "
            f"(snap {snap}, cur {cur})")

    def op(self, agent, fn, reads=(), writes=(), inc=True, dma=False, coll=False,
           tag=""):
        waits = {}

        def need(sem, cnt, snap, rname=""):
            if sem == agent and cnt is None:
                return
            c = self._resolve(sem, cnt, snap, tag + "/" + rname)
            if self.sat[agent][sem] >= c:
                return
            waits[sem] = max(waits.get(sem, 0), c)

        for r in reads:
            ent = self.res.get(r)
            if ent is None:
                raise RuntimeError(f"{tag}: read of never-written resource {r}")
            for sem, (cnt, snap) in ent["w"].items():
                need(sem, cnt, snap, r)
        for r in writes:
            ent = self.res.get(r)
            if ent is not None:
                for sem, (cnt, snap) in ent["w"].items():
                    need(sem, cnt, snap, r)
                for sem, (cnt, snap) in ent["r"].items():
                    need(sem, cnt, snap, r)
        for sem in sorted(waits):
            self.sat[agent][sem] = waits[sem]
            self.q[agent].append(("wait", sem, waits[sem]))

        if coll:
            dsem = "dcoll"
            if self.cnt[dsem] > self.sat[agent][dsem]:
                self.sat[agent][dsem] = self.cnt[dsem]
                self.q[agent].append(("wait", dsem, self.cnt[dsem]))
            self.cnt[dsem] += 1
            eff_sem, eff_cnt = dsem, self.cnt[dsem]
        elif dma:
            q = DMA_OF[agent]
            idx = self.dmarr[q] % DMA_POOL
            self.dmarr[q] += 1
            dsem = f"{q}{idx}"
            if self.cnt[dsem] > self.sat[agent][dsem]:
                self.sat[agent][dsem] = self.cnt[dsem]
                self.q[agent].append(("wait", dsem, self.cnt[dsem]))
            self.cnt[dsem] += 16
            eff_sem, eff_cnt = dsem, self.cnt[dsem]
        elif inc:
            self.cnt[agent] += 1
            eff_sem, eff_cnt = agent, self.cnt[agent]
        else:
            eff_sem, eff_cnt = agent, None
        snap = self.cnt[eff_sem]
        for r in writes:
            self.res[r] = {"w": {eff_sem: (eff_cnt, snap)}, "r": {}}
        for r in reads:
            rr = self.res[r]["r"]
            old = rr.get(eff_sem)
            if old is None or old[0] is None or (eff_cnt is not None and eff_cnt > old[0]):
                rr[eff_sem] = (eff_cnt, snap)
        emit_eff = (eff_sem, eff_cnt) if (dma or coll or (inc and eff_cnt is not None)) else None
        self.q[agent].append(("op", fn, emit_eff))

    def barrier(self):
        snap = dict(self.cnt)
        for a in COMPUTE_AGENTS:
            for t in COMPUTE_AGENTS + DMA_AGENTS:
                if snap[t] > self.sat[a][t]:
                    self.sat[a][t] = snap[t]
                    self.q[a].append(("wait", t, snap[t]))

    def emit(self, stack):
        nc = self.nc
        for s in COMPUTE_AGENTS + DMA_AGENTS:
            self.sems[s] = stack.enter_context(nc.semaphore("sem_" + s))
        block = stack.enter_context(nc.Block())
        sems = self.sems

        def runner(agent):
            def run(engine):
                for item in self.q[agent]:
                    if item[0] == "wait":
                        engine.wait_ge(sems[item[1]], item[2])
                    else:
                        _, fn, eff = item
                        inst = fn(engine)
                        if eff is not None:
                            sem, cnt = eff
                            inst.then_inc(sems[sem],
                                          16 if (sem in DMA_AGENTS and sem != "dcoll")
                                          else 1)
            return run

        block.sync(runner("sp"))
        block.vector(runner("dve"))
        block.scalar(runner("act"))
        block.tensor(runner("pe"))
        block.gpsimd(runner("pool"))


def cdiv(a, b):
    return (a + b - 1) // b


def build_program(T=2048, want_p=[None]):
    assert T % 512 == 0, T
    NCH = T // 128
    TH = T + 3
    NTC = cdiv(TH, 512)
    TCW = cdiv(TH, NTC)
    tc_ranges = [(i * TCW, min(TH, (i + 1) * TCW)) for i in range(NTC)]
    NTT = cdiv(TH, 128)
    NKD = INTER // 128

    nc = bass.Bass("TRN2", target_bir_lowering=False)

    x_in = nc.dram_tensor("x", [TH, DM], F32, kind="ExternalInput").ap()
    w_in = nc.dram_tensor("in_proj_w", [DPROJ, DM], F32, kind="ExternalInput").ap()
    conv_w = nc.dram_tensor("conv_w", [CONVD, CONVK], F32, kind="ExternalInput").ap()
    conv_b = nc.dram_tensor("conv_b", [CONVD, 1], F32, kind="ExternalInput").ap()
    dt_bias = nc.dram_tensor("dt_bias", [NH, 1], F32, kind="ExternalInput").ap()
    a_log = nc.dram_tensor("A_log", [NH, 1], F32, kind="ExternalInput").ap()
    d_in = nc.dram_tensor("D", [1, NH], F32, kind="ExternalInput").ap()
    norm_w = nc.dram_tensor("norm_w", [1, DIN], F32, kind="ExternalInput").ap()
    w_out = nc.dram_tensor("out_proj_w", [DM, DIN], F32, kind="ExternalInput").ap()
    g_qw = nc.dram_tensor("gate_qweight", [DM // 8, INTER], I32, kind="ExternalInput").ap()
    g_sr = nc.dram_tensor("gate_scales_rep", [DM // 8, INTER], F16, kind="ExternalInput").ap()
    g_s = nc.dram_tensor("gate_scales", [32, INTER], F16, kind="ExternalInput").ap()
    g_z = nc.dram_tensor("gate_zeros", [32, INTER // 8], I32, kind="ExternalInput").ap()
    u_qw = nc.dram_tensor("up_qweight", [DM // 8, INTER], I32, kind="ExternalInput").ap()
    u_sr = nc.dram_tensor("up_scales_rep", [DM // 8, INTER], F16, kind="ExternalInput").ap()
    u_s = nc.dram_tensor("up_scales", [32, INTER], F16, kind="ExternalInput").ap()
    u_z = nc.dram_tensor("up_zeros", [32, INTER // 8], I32, kind="ExternalInput").ap()
    d_qw = nc.dram_tensor("down_qweight", [INTER // 8, DM], I32, kind="ExternalInput").ap()
    d_sr = nc.dram_tensor("down_scales_rep", [INTER // 8, DM], F16, kind="ExternalInput").ap()
    d_s = nc.dram_tensor("down_scales", [NKD, DM], F16, kind="ExternalInput").ap()
    d_z = nc.dram_tensor("down_zeros", [NKD, DM // 8], I32, kind="ExternalInput").ap()
    eye_bf_c = nc.dram_tensor("eye_bf", [128, 128], BF16, kind="ExternalInput").ap()
    eye_f32_c = nc.dram_tensor("eye_f32", [128, 128], F32, kind="ExternalInput").ap()
    tri_cst = nc.dram_tensor("tri", [128, 128], mybir.dt.uint8, kind="ExternalInput").ap()
    ones_row_c = nc.dram_tensor("ones_row", [1, 128], F32, kind="ExternalInput").ap()
    selw_cst = nc.dram_tensor("selw", [9, 8], F32, kind="ExternalInput").ap()

    out_t = nc.dram_tensor("out", [T, DM], F32, kind="ExternalOutput").ap()

    w_in_bf = nc.dram_tensor("w_in_bf", [DPROJ, DM], BF16).ap()
    w_out_bf = nc.dram_tensor("w_out_bf", [DM, DIN], BF16).ap()
    xT_sp = nc.dram_tensor("xT_sp", [DM, T], F32).ap()
    xbcT_sp = nc.dram_tensor("xbcT_sp", [CONVD, T], BF16).ap()
    zT_sp = nc.dram_tensor("zT_sp", [DIN, T], BF16).ap()
    csf_sp = nc.dram_tensor("csf_sp", [NH, T], F32).ap()
    dexp_sp = nc.dram_tensor("dexp_sp", [NCH, NH], F32).ap()
    sfin_sp = nc.dram_tensor("sfin_sp", [DS, DIN], F32).ap()
    datot_sp = nc.dram_tensor("datot_sp", [1, NH], F32).ap()
    sg_ag = nc.dram_tensor("sg_ag", [NCORES * DS, DIN], F32).ap()
    dat_ag = nc.dram_tensor("dat_ag", [NCORES, NH], F32).ap()
    wrow_sp = nc.dram_tensor("wrow_sp", [NCORES, NH], F32).ap()
    yn_sp = nc.dram_tensor("yn_sp", [T, DIN], BF16).ap()
    x1T_sp = nc.dram_tensor("x1T_sp", [DM, T], F32).ap()
    x2T_sp = nc.dram_tensor("x2T_sp", [DM, T], F16).ap()
    wg_dq = nc.dram_tensor("wg_dq", [DM, INTER], F16).ap()
    wu_dq = nc.dram_tensor("wu_dq", [DM, INTER], F16).ap()
    wd_dq = nc.dram_tensor("wd_dq", [INTER, DM], F16).ap()
    xsg_sp = nc.dram_tensor("xsg_sp", [32, T], F32).ap()
    xsd_sp = nc.dram_tensor("xsd_sp", [NKD, T], F32).ap()
    outT_sp = nc.dram_tensor("outT_sp", [DM, T], F32).ap()

    p = P(nc)
    want_p[0] = p
    stack = ExitStack()
    E = stack.enter_context

    def dma(agent, dst, src, reads=(), writes=(), tag=""):
        p.op(agent, lambda en, d=dst, s=src: en.dma_start(out=d, in_=s),
             reads=reads, writes=writes, dma=True, tag=tag)

    def dmaT(agent, dst, src, reads=(), writes=(), tag=""):
        p.op(agent, lambda en, d=dst, s=src: en.dma_start(out=d, in_=s, transpose=True),
             reads=reads, writes=writes, dma=True, tag=tag)

    with stack:
        c_eye_bf = E(nc.sbuf_tensor("c_eye_bf", [128, 128], BF16))
        c_eye_f32 = E(nc.sbuf_tensor("c_eye_f32", [128, 128], F32))
        c_tri = E(nc.sbuf_tensor("c_tri", [128, 128], mybir.dt.uint8))
        c_neg = E(nc.sbuf_tensor("c_neg", [128, 128], F32))
        c_dtb = E(nc.sbuf_tensor("c_dtb", [128, 1], F32))
        c_alog = E(nc.sbuf_tensor("c_alog", [128, 1], F32))
        c_A = E(nc.sbuf_tensor("c_A", [128, 1], F32))
        c_D = E(nc.sbuf_tensor("c_D", [128, 128], F32))
        c_or = E(nc.sbuf_tensor("c_or", [1, 128], F32))
        c_eps = E(nc.sbuf_tensor("c_eps", [128, 1], F32))
        c_one1 = E(nc.sbuf_tensor("c_one1", [128, 1], F32))
        c_oc = E(nc.sbuf_tensor("c_oc", [128, 1], F32))
        c_ocf16 = E(nc.sbuf_tensor("c_ocf16", [128, 1], F16))
        c_selw = E(nc.sbuf_tensor("c_selw", [9, 8], F32))
        dt_sb = E(nc.sbuf_tensor("dt_sb", [128, T], F32))

        dma("sp", c_eye_bf[:], eye_bf_c, writes=("c_eye_bf",))
        dma("sp", c_eye_f32[:], eye_f32_c, writes=("c_eye_f32",))
        dma("sp", c_tri[:], tri_cst, writes=("c_tri",))
        dma("sp", c_dtb[:], dt_bias, writes=("c_dtb",))
        dma("sp", c_alog[:], a_log, writes=("c_alog",))
        dma("sp", c_or[:], ones_row_c, writes=("c_or",))
        dma("sp", c_selw[:], selw_cst, writes=("c_selw",))
        dma("pool", c_D[:], bass.AP(d_in.tensor, 0, [[0, 128], [1, 128]]),
            writes=("c_D",))
        p.op("dve", lambda en: en.memset(c_neg[:], -1e30), writes=("c_neg",))
        p.op("dve", lambda en: en.memset(c_eps[:], float(EPS)), writes=("c_eps",))
        p.op("dve", lambda en: en.memset(c_one1[:], 1.0), writes=("c_one1",))
        p.op("dve", lambda en: en.memset(c_oc[:], 1.0), writes=("c_oc",))
        p.op("dve", lambda en: en.tensor_copy(out=c_ocf16[:], in_=c_oc[:]),
             reads=("c_oc",), writes=("c_ocf16",))
        p.op("act", lambda en: en.activation(out=c_A[:], in_=c_alog[:], func=AF.Exp),
             reads=("c_alog",), writes=("c_A",))
        p.op("dve", lambda en: en.tensor_scalar(out=c_A[:], in0=c_A[:], scalar1=-1.0,
                                                scalar2=None, op0=ALU.mult),
             reads=("c_A",), writes=("c_A",))

        WINC = 1160
        nwc = cdiv(DPROJ, WINC)
        for i in range(nwc):
            r0, r1 = i * WINC, min(DPROJ, (i + 1) * WINC)
            dma("pool", w_in_bf[r0:r1, :], w_in[r0:r1, :], writes=(f"wincast/{i}",))
        for i in range(8):
            dma("pool", w_out_bf[i * 512:(i + 1) * 512, :], w_out[i * 512:(i + 1) * 512, :],
                writes=(f"woutcast/{i}",))

        ln1_scope = ExitStack()
        ln1T = ln1_scope.enter_context(nc.sbuf_tensor("ln1T", [128, 32 * TH], BF16))
        _sub = ExitStack()
        px = _sub.enter_context(nc.sbuf_tensor("px", [128, DM], F32))
        pstat = _sub.enter_context(nc.sbuf_tensor("pstat", [128, 8], F32))
        pln = _sub.enter_context(nc.sbuf_tensor("pln", [128, DM], F32))
        pxl = _sub.enter_context(nc.sbuf_tensor("pxl", [128, DM], F32))
        pxs = _sub.enter_context(nc.sbuf_tensor("pxs", [128, DM], F32))
        ptP = _sub.enter_context(nc.psum_tensor("ptP", [128, 128], F32))
        ptPf = _sub.enter_context(nc.psum_tensor("ptPf", [128, 128], F32))
        if True:
            for tt in range(NTT):
                t0 = tt * 128
                t1 = min(TH, t0 + 128)
                w = t1 - t0
                dma("sp", px[0:w, :], x_in[t0:t1, :], reads=("pxs_done",) if tt else (),
                    writes=("px",))
                p.op("dve", lambda en, w=w: en.reduce_sum(
                    out=pstat[0:w, 0:1], in_=px[0:w, :], axis=AX.X),
                    reads=("px",), writes=("pst0",))
                p.op("act", lambda en, w=w: en.activation(
                    out=pxs[0:w, :], in_=px[0:w, :], func=AF.Square,
                    accum_out=pstat[0:w, 1:2]),
                    reads=("px",), writes=("pst1", "pxs_sq"))
                p.op("dve", lambda en, w=w: en.tensor_scalar(
                    out=pstat[0:w, 2:3], in0=pstat[0:w, 0:1], scalar1=1.0 / DM,
                    scalar2=None, op0=ALU.mult), reads=("pst0",), writes=("pm",))
                p.op("dve", lambda en, w=w: en.tensor_tensor(
                    out=pstat[0:w, 3:4], in0=pstat[0:w, 2:3], in1=pstat[0:w, 2:3],
                    op=ALU.mult), reads=("pm",), writes=("pmsq",))
                p.op("dve", lambda en, w=w: en.scalar_tensor_tensor(
                    out=pstat[0:w, 4:5], in0=pstat[0:w, 1:2], scalar=1.0 / DM,
                    in1=pstat[0:w, 3:4], op0=ALU.mult, op1=ALU.subtract),
                    reads=("pst1", "pmsq"), writes=("pvar",))
                p.op("act", lambda en, w=w: en.activation(
                    out=pstat[0:w, 5:6], in_=pstat[0:w, 4:5], func=AF.Sqrt,
                    bias=c_eps[0:w, 0:1]), reads=("pvar", "c_eps"), writes=("psd",))
                p.op("dve", lambda en, w=w: en.reciprocal(
                    out=pstat[0:w, 6:7], in_=pstat[0:w, 5:6]),
                    reads=("psd",), writes=("prstd",))
                p.op("dve", lambda en, w=w: en.tensor_scalar(
                    out=pln[0:w, :], in0=px[0:w, :], scalar1=pstat[0:w, 2:3],
                    scalar2=pstat[0:w, 6:7], op0=ALU.subtract, op1=ALU.mult),
                    reads=("px", "pm", "prstd"), writes=("pln",))
                p.op("dve", lambda en, w=w: en.tensor_tensor(
                    out=pxl[0:w, :], in0=px[0:w, :], in1=pln[0:w, :], op=ALU.add),
                    reads=("px", "pln", "pxs_sq"), writes=("pxl",))
                for dtl in range(32):
                    d0 = dtl * 128
                    p.op("pe", lambda en, w=w, d0=d0: en.transpose(
                        out=ptP[:, 0:w], in_=pln[0:w, d0:d0 + 128],
                        identity=c_eye_f32[0:w, 0:w]),
                        reads=("pln", "c_eye_f32"), writes=("ptP",))
                    p.op("dve", lambda en, w=w, dtl=dtl, t0=t0: en.tensor_copy(
                        out=ln1T[:, dtl * TH + t0: dtl * TH + t0 + w], in_=ptP[:, 0:w]),
                        reads=("ptP",), writes=("ln1T",))
                    p.op("pe", lambda en, w=w, d0=d0: en.transpose(
                        out=ptPf[:, 0:w], in_=pxl[0:w, d0:d0 + 128],
                        identity=c_eye_f32[0:w, 0:w]),
                        reads=("pxl", "c_eye_f32"), writes=("ptPf",))
                    p.op("dve", lambda en, w=w, d0=d0: en.tensor_copy(
                        out=pxs[:, d0:d0 + w], in_=ptPf[:, 0:w]),
                        reads=("ptPf",), writes=("pxs",))
                tok0 = t0 - 3
                lo = max(0, tok0)
                hi = min(T, t1 - 3)
                if hi > lo:
                    off = lo - tok0
                    nwd = hi - lo
                    src = bass.AP(pxs.ap().tensor, off,
                                  [[DM, 128], [128, 32], [1, nwd]])
                    dst = bass.AP(xT_sp.tensor, lo,
                                  [[T, 128], [128 * T, 32], [1, nwd]])
                    dma("act", dst, src, reads=("pxs",), writes=("xT_sp", "pxs_done"))
                else:
                    p.op("dve", lambda en: en.engine_nop(), reads=("pxs",),
                         writes=("pxs_done",))
        _sub.close()
        p.barrier()

        FT = DPROJ // 128
        NXB = CONVD // 128
        DT_FT = FT - 1
        _sub = ExitStack()
        wsl = _sub.enter_context(nc.sbuf_tensor("wsl", [128, 2 * 32 * 256], BF16))
        evf = _sub.enter_context(nc.sbuf_tensor("evf", [128, 2 * 512], F32))
        evb = _sub.enter_context(nc.sbuf_tensor("evb", [128, 2 * 512], BF16))
        cstg = _sub.enter_context(nc.sbuf_tensor("cstg", [128, 2 * 520], F32))
        haloB = _sub.enter_context(nc.sbuf_tensor("halo", [128, 3 * NXB], F32))
        cwt = _sub.enter_context(nc.sbuf_tensor("cwt", [128, CONVK * NXB], F32))
        cbt = _sub.enter_context(nc.sbuf_tensor("cbt", [128, NXB], F32))
        sp1 = _sub.enter_context(nc.sbuf_tensor("sp1", [128, 512], F32))
        sp2 = _sub.enter_context(nc.sbuf_tensor("sp2", [128, 512], F32))
        ipA = _sub.enter_context(nc.psum_tensor("ipA", [128, 512], F32))
        ipB = _sub.enter_context(nc.psum_tensor("ipB", [128, 512], F32))
        if True:
            p.op("dve", lambda en: en.memset(haloB[:], 0.0),
                 writes=tuple(f"halo/{i}" for i in range(NXB)))
            for i in range(NXB):
                dma("sp", cwt[:, i * CONVK:(i + 1) * CONVK],
                    conv_w[i * 128:(i + 1) * 128, :], writes=(f"cw/{i}",))
                dma("sp", cbt[:, i:i + 1], conv_b[i * 128:(i + 1) * 128, :],
                    writes=(f"cb/{i}",))

            ip_ps = [ipA, ipB]
            gidx = 0

            def evac_A(ft, a, b, ps, psres):
                nonlocal gidx
                tw = b - a
                slot = gidx % 2
                if ft < DIN // 128:
                    lo = max(a, 3)
                    if lo >= b:
                        return
                    co = lo - a
                    nw = b - lo
                    p.op("act", lambda en, ps=ps, co=co, nw=nw, slot=slot:
                         en.activation(out=evb[:, slot * 512: slot * 512 + nw],
                                       in_=ps[:, co:co + nw], func=AF.Silu),
                         reads=(psres,), writes=(f"evb/{slot}",))
                    dma("act", zT_sp[ft * 128:(ft + 1) * 128, lo - 3: b - 3],
                        evb[:, slot * 512: slot * 512 + nw],
                        reads=(f"evb/{slot}",), writes=("zT_sp",))
                elif ft < DT_FT:
                    i = ft - DIN // 128
                    p.op("dve", lambda en, i=i, slot=slot: en.tensor_copy(
                        out=cstg[:, slot * 520: slot * 520 + 3],
                        in_=haloB[:, i * 3:(i + 1) * 3]),
                        reads=(f"halo/{i}",), writes=(f"cstg/{slot}",))
                    p.op("dve", lambda en, ps=ps, tw=tw, slot=slot: en.tensor_copy(
                        out=cstg[:, slot * 520 + 3: slot * 520 + 3 + tw],
                        in_=ps[:, 0:tw]),
                        reads=(psres,), writes=(f"cstg/{slot}",))
                    p.op("dve", lambda en, ps=ps, tw=tw, i=i: en.tensor_copy(
                        out=haloB[:, i * 3:(i + 1) * 3], in_=ps[:, tw - 3:tw]),
                        reads=(psres,), writes=(f"halo/{i}",))
                    lo = max(0, a - 3)
                    wo = (b - 3) - lo
                    base = lo - (a - 3)
                    p.op("dve", lambda en, i=i, slot=slot, base=base, wo=wo:
                         en.tensor_scalar(
                             out=evf[:, slot * 512: slot * 512 + wo],
                             in0=cstg[:, slot * 520 + base: slot * 520 + base + wo],
                             scalar1=cwt[:, i * CONVK: i * CONVK + 1], scalar2=None,
                             op0=ALU.mult),
                         reads=(f"cstg/{slot}", f"cw/{i}"), writes=(f"evf/{slot}",))
                    for k in range(1, CONVK):
                        p.op("dve", lambda en, i=i, k=k, slot=slot, base=base, wo=wo:
                             en.scalar_tensor_tensor(
                                 out=evf[:, slot * 512: slot * 512 + wo],
                                 in0=cstg[:, slot * 520 + base + k:
                                          slot * 520 + base + k + wo],
                                 scalar=cwt[:, i * CONVK + k: i * CONVK + k + 1],
                                 in1=evf[:, slot * 512: slot * 512 + wo],
                                 op0=ALU.mult, op1=ALU.add),
                             reads=(f"cstg/{slot}", f"cw/{i}", f"evf/{slot}"),
                             writes=(f"evf/{slot}",))
                    p.op("act", lambda en, i=i, slot=slot, wo=wo: en.activation(
                        out=evb[:, slot * 512: slot * 512 + wo],
                        in_=evf[:, slot * 512: slot * 512 + wo], func=AF.Silu,
                        bias=cbt[:, i:i + 1]),
                        reads=(f"evf/{slot}", f"cb/{i}"), writes=(f"evb/{slot}",))
                    dma("act", xbcT_sp[i * 128:(i + 1) * 128, lo:lo + wo],
                        evb[:, slot * 512: slot * 512 + wo],
                        reads=(f"evb/{slot}",), writes=("xbcT_sp",))
                else:
                    lo = max(a, 3)
                    if lo >= b:
                        return
                    co = lo - a
                    nw = b - lo
                    tok = lo - 3
                    sl = slot * 512
                    p.op("dve", lambda en, ps=ps, co=co, nw=nw, sl=sl:
                         en.tensor_scalar(out=evf[:, sl:sl + nw],
                                          in0=ps[:, co:co + nw],
                                          scalar1=c_dtb[:, 0:1], scalar2=None,
                                          op0=ALU.add),
                         reads=(psres, "c_dtb"), writes=(f"evf/{slot}",))
                    p.op("dve", lambda en, nw=nw, sl=sl: en.tensor_scalar(
                        out=sp1[:, 0:nw], in0=evf[:, sl:sl + nw], scalar1=-1.0,
                        scalar2=None, op0=ALU.mult),
                        reads=(f"evf/{slot}",), writes=("sp1",))
                    p.op("dve", lambda en, nw=nw, sl=sl: en.tensor_tensor(
                        out=sp1[:, 0:nw], in0=evf[:, sl:sl + nw], in1=sp1[:, 0:nw],
                        op=ALU.max),
                        reads=(f"evf/{slot}", "sp1"), writes=("sp1",))
                    p.op("act", lambda en, nw=nw: en.activation(
                        out=sp2[:, 0:nw], in_=sp1[:, 0:nw], func=AF.Exp, scale=-1.0),
                        reads=("sp1",), writes=("sp2",))
                    p.op("act", lambda en, nw=nw: en.activation(
                        out=sp2[:, 0:nw], in_=sp2[:, 0:nw], func=AF.Ln,
                        bias=c_one1[:, 0:1]),
                        reads=("sp2", "c_one1"), writes=("sp2",))
                    p.op("dve", lambda en, nw=nw, sl=sl: en.tensor_scalar(
                        out=sp1[:, 0:nw], in0=evf[:, sl:sl + nw], scalar1=0.0,
                        scalar2=None, op0=ALU.max),
                        reads=(f"evf/{slot}",), writes=("sp1",))
                    p.op("dve", lambda en, nw=nw, tok=tok: en.tensor_tensor(
                        out=dt_sb[:, tok:tok + nw], in0=sp1[:, 0:nw],
                        in1=sp2[:, 0:nw], op=ALU.add),
                        reads=("sp1", "sp2"), writes=("dt_sb",))

            for fs in range(cdiv(FT, 2)):
                fts = [ft for ft in (2 * fs, 2 * fs + 1) if ft < FT]
                slot = fs % 2
                wres = f"wsl/{slot}"
                nf = len(fts) * 128
                castdep = tuple({f"wincast/{(fts[0] * 128) // WINC}",
                                 f"wincast/{(fts[0] * 128 + nf - 1) // WINC}"})
                for dtl in range(32):
                    dmaT("sp",
                         bass.AP(wsl.ap().tensor, slot * 8192 + dtl * 256,
                                 [[2 * 32 * 256, 128], [1, nf]]),
                         w_in_bf[fts[0] * 128: fts[0] * 128 + nf,
                                 dtl * 128:(dtl + 1) * 128],
                         reads=castdep, writes=(wres,), tag="wslT")
                for fi, ft in enumerate(fts):
                    for (a, b) in tc_ranges:
                        tw = b - a
                        ps = ip_ps[gidx % 2]
                        psres = f"ip_ps/{gidx % 2}"
                        for dtl in range(32):
                            last = dtl == 31
                            p.op("pe", lambda en, dtl=dtl, fi=fi, a=a, tw=tw, ps=ps,
                                 slot=slot, lst=last: en.matmul(
                                     out=ps[:, 0:tw],
                                     lhsT=wsl[:, slot * 8192 + dtl * 256 + fi * 128:
                                              slot * 8192 + dtl * 256 + fi * 128 + 128],
                                     rhs=ln1T[:, dtl * TH + a: dtl * TH + a + tw],
                                     start=(dtl == 0), stop=lst),
                                 reads=(wres, "ln1T"),
                                 writes=(psres,) if (last or dtl == 0) else (),
                                 inc=last, tag="ipmm")
                        evac_A(ft, a, b, ps, psres)
                        gidx += 1
        _sub.close()
        ln1_scope.close()
        p.barrier()

        b_scope = ExitStack()
        Eb = b_scope.enter_context
        cs_sb = Eb(nc.sbuf_tensor("cs_sb", [128, T], F32))
        sruns = Eb(nc.sbuf_tensor("sruns", [128, DIN], F32))
        dexp_all = Eb(nc.sbuf_tensor("dexp_all", [128, NCH * 128], F32))
        _sub = ExitStack()
        dA_t = _sub.enter_context(nc.sbuf_tensor("dA_t", [128, T], F32))
        ones_t = _sub.enter_context(nc.sbuf_tensor("ones_t", [128, 128], F32))
        dtot = _sub.enter_context(nc.sbuf_tensor("dtot", [128, NCH], F32))
        dtotT = _sub.enter_context(nc.sbuf_tensor("dtotT", [128, 128], F32))
        datot1 = _sub.enter_context(nc.sbuf_tensor("datot1", [128, 1], F32))
        pTt = _sub.enter_context(nc.psum_tensor("pTt", [128, 128], F32))
        if True:
            p.op("dve", lambda en: en.memset(ones_t[:], 1.0), writes=("ones_t",))
            p.op("dve", lambda en: en.tensor_scalar(
                out=dA_t[:], in0=dt_sb[:], scalar1=c_A[:, 0:1], scalar2=None,
                op0=ALU.mult), reads=("dt_sb", "c_A"), writes=("dA_t",))
            for c in range(NCH):
                p.op("dve", lambda en, c=c: en.tensor_tensor_scan(
                    out=cs_sb[:, c * 128:(c + 1) * 128], data0=ones_t[:],
                    data1=dA_t[:, c * 128:(c + 1) * 128], initial=0.0,
                    op0=ALU.mult, op1=ALU.add),
                    reads=("ones_t", "dA_t"), writes=("cs_sb",))
                p.op("dve", lambda en, c=c: en.tensor_copy(
                    out=dtot[:, c:c + 1], in_=cs_sb[:, c * 128 + 127: c * 128 + 128]),
                    reads=("cs_sb",), writes=("dtot",))
            p.op("act", lambda en: en.activation(out=dtot[:], in_=dtot[:], func=AF.Exp),
                 reads=("dtot",), writes=("dtot",))
            p.op("pe", lambda en: en.transpose(out=pTt[0:NCH, 0:128], in_=dtot[:],
                                               identity=c_eye_f32[:]),
                 reads=("dtot", "c_eye_f32"), writes=("pTt",))
            p.op("dve", lambda en: en.tensor_copy(out=dtotT[0:NCH, :],
                                                  in_=pTt[0:NCH, 0:128]),
                 reads=("pTt",), writes=("dtotT",))
            dma("act", dexp_sp[:, :], dtotT[0:NCH, :], reads=("dtotT",),
                writes=("dexp_sp",))
            dma("act", bass.AP(dexp_all.ap().tensor, 0, [[NCH * 128, 128], [1, NCH * 128]]),
                bass.AP(dexp_sp.tensor, 0, [[0, 128], [1, NCH * 128]]),
                reads=("dexp_sp",), writes=("dexp_all",))
            p.op("dve", lambda en: en.reduce_sum(out=datot1[:], in_=dA_t[:], axis=AX.X),
                 reads=("dA_t",), writes=("datot1",))
            dma("act", bass.AP(datot_sp.tensor, 0, [[1, 128], [1, 1]]), datot1[:],
                reads=("datot1",), writes=("datot_sp",))
            dma("act", csf_sp[:, :], cs_sb[:], reads=("cs_sb",), writes=("csf_sp",))
        _sub.close()
        p.barrier()

        x_tok = Eb(nc.sbuf_tensor("x_tok", [128, DIN], BF16))
        xw_tok = Eb(nc.sbuf_tensor("xw_tok", [128, DIN], BF16))
        b_tok = Eb(nc.sbuf_tensor("b_tok", [128, NG * DS], BF16))
        tT = Eb(nc.sbuf_tensor("tT", [128, 384], F32))
        wv_sb = Eb(nc.sbuf_tensor("wv_sb", [128, 128], F32))
        dec_sb = Eb(nc.sbuf_tensor("dec_sb", [128, 128], F32))
        od_sb = Eb(nc.sbuf_tensor("od_sb", [128, 128], F32))
        pT3 = Eb(nc.psum_tensor("pT3", [128, 384], F32))

        def chunk_prep(c, need_od):
            c0 = c * 128
            dmaT("sp", x_tok[:], xbcT_sp[0:DIN, c0:c0 + 128],
                 reads=("xbcT_sp",), writes=("x_tok",), tag="xtokT")
            dmaT("sp", b_tok[:], xbcT_sp[DIN:DIN + NG * DS, c0:c0 + 128],
                 reads=("xbcT_sp",), writes=("b_tok",), tag="btokT")
            p.op("act", lambda en, c0=c0: en.activation(
                out=dec_sb[:], in_=cs_sb[:, c0:c0 + 128], func=AF.Exp, scale=-1.0,
                bias=cs_sb[:, c0 + 127:c0 + 128]),
                reads=("cs_sb",), writes=("dec_sb",))
            if need_od:
                p.op("act", lambda en, c0=c0: en.activation(
                    out=od_sb[:], in_=cs_sb[:, c0:c0 + 128], func=AF.Exp),
                    reads=("cs_sb",), writes=("od_sb",))
            p.op("pe", lambda en, c0=c0: en.transpose(
                out=pT3[:, 0:128], in_=dt_sb[:, c0:c0 + 128], identity=c_eye_f32[:]),
                reads=("dt_sb", "c_eye_f32"), writes=("pT3a",), tag="dtT")
            p.op("pe", lambda en: en.transpose(
                out=pT3[:, 128:256], in_=dec_sb[:], identity=c_eye_f32[:]),
                reads=("dec_sb", "c_eye_f32"), writes=("pT3b",), tag="decT")
            if need_od:
                p.op("pe", lambda en: en.transpose(
                    out=pT3[:, 256:384], in_=od_sb[:], identity=c_eye_f32[:]),
                    reads=("od_sb", "c_eye_f32"), writes=("pT3c",), tag="odT")
            allt = ("pT3a", "pT3b") + (("pT3c",) if need_od else ())
            p.op("dve", lambda en: en.tensor_copy(out=tT[:, 0:128], in_=pT3[:, 0:128]),
                 reads=allt, writes=("tT_dt",))
            p.op("dve", lambda en: en.tensor_copy(out=tT[:, 128:256],
                                                  in_=pT3[:, 128:256]),
                 reads=allt, writes=("tT_dec",))
            if need_od:
                p.op("dve", lambda en: en.tensor_copy(out=tT[:, 256:384],
                                                      in_=pT3[:, 256:384]),
                     reads=allt, writes=("tT_od",))
            p.op("dve", lambda en: en.tensor_tensor(
                out=wv_sb[:], in0=tT[:, 0:128], in1=tT[:, 128:256], op=ALU.mult),
                reads=("tT_dt", "tT_dec"), writes=("wv_sb",))
            for g in range(NG):
                wvv = bass.AP(wv_sb.ap().tensor, g * 16, [[128, 128], [1, 16], [0, 64]])
                p.op("dve", lambda en, g=g, wvv=wvv: en.tensor_tensor(
                    out=xw_tok[:, g * 1024:(g + 1) * 1024],
                    in0=x_tok[:, g * 1024:(g + 1) * 1024], in1=wvv, op=ALU.mult),
                    reads=("x_tok", "wv_sb"), writes=("xw_tok",))

        def states_and_update(c, psS, sl):
            for g in range(NG):
                for h2 in range(2):
                    p.op("pe", lambda en, g=g, h2=h2, psS=psS: en.matmul(
                        out=psS[:, h2 * 512:(h2 + 1) * 512],
                        lhsT=b_tok[:, g * 128:(g + 1) * 128],
                        rhs=xw_tok[:, g * 1024 + h2 * 512: g * 1024 + (h2 + 1) * 512],
                        start=True, stop=True),
                        reads=("b_tok", "xw_tok"),
                        writes=sl(g, h2), inc=True, tag="statesmm")
                dv = bass.AP(dexp_all.ap().tensor, c * 128 + g * 16,
                             [[NCH * 128, 128], [1, 16], [0, 64]])
                p.op("dve", lambda en, g=g, dv=dv: en.tensor_tensor(
                    out=sruns[:, g * 1024:(g + 1) * 1024],
                    in0=sruns[:, g * 1024:(g + 1) * 1024], in1=dv, op=ALU.mult),
                    reads=("sruns", "dexp_all"), writes=("sruns",))
                for h2 in range(2):
                    p.op("dve", lambda en, g=g, h2=h2, psS=psS: en.tensor_tensor(
                        out=sruns[:, g * 1024 + h2 * 512: g * 1024 + (h2 + 1) * 512],
                        in0=sruns[:, g * 1024 + h2 * 512: g * 1024 + (h2 + 1) * 512],
                        in1=psS[:, h2 * 512:(h2 + 1) * 512], op=ALU.add),
                        reads=("sruns",) + sl(g, h2), writes=("sruns",))

        with nc.psum_tensor("psS1", [128, 1024], F32) as psS1:
            p.op("dve", lambda en: en.memset(sruns[:], 0.0), writes=("sruns",))
            for c in range(NCH):
                chunk_prep(c, need_od=False)
                states_and_update(c, psS1, lambda g, h2: (f"psS1/{h2}",))
            dma("act", sfin_sp[:, :], sruns[:], reads=("sruns",), writes=("sfin_sp",))
        p.barrier()
        p.op("pool", lambda en: en.collective_compute(
            "AllGather", ALU.bypass, replica_groups=[list(range(NCORES))],
            ins=[sfin_sp.opt()], outs=[sg_ag.opt()]),
            reads=("sfin_sp",), writes=("sg_ag",), coll=True, tag="ag1")
        p.op("pool", lambda en: en.collective_compute(
            "AllGather", ALU.bypass, replica_groups=[list(range(NCORES))],
            ins=[datot_sp.opt()], outs=[dat_ag.opt()]),
            reads=("datot_sp",), writes=("dat_ag",), coll=True, tag="ag2")

        _sub = ExitStack()
        datg = _sub.enter_context(nc.sbuf_tensor("datg", [9, 128], F32))
        wcomb = _sub.enter_context(nc.sbuf_tensor("wcomb", [8, 128], F32))
        wbc = _sub.enter_context(nc.sbuf_tensor("wbc", [128, 128], F32))
        sjb = _sub.enter_context(nc.sbuf_tensor("sjb", [128, DIN], F32))
        tmpc = _sub.enter_context(nc.sbuf_tensor("tmpc", [128, 1024], F32))
        pW = _sub.enter_context(nc.psum_tensor("pW", [8, 128], F32))
        if True:
            p.op("dve", lambda en: en.memset(datg[:], 1.0), writes=("datg",))
            dma("sp", datg[0:8, :], dat_ag, reads=("dat_ag", "datg"), writes=("datg",))
            p.op("pe", lambda en: en.matmul(out=pW[:], lhsT=c_selw[:], rhs=datg[:],
                                            start=True, stop=True),
                 reads=("c_selw", "datg"), writes=("pW",), tag="wcombmm")
            p.op("act", lambda en: en.activation(out=wcomb[:], in_=pW[:], func=AF.Exp),
                 reads=("pW",), writes=("wcomb",))
            dma("act", wrow_sp[:, :], wcomb[:], reads=("wcomb",), writes=("wrow_sp",))
            p.op("dve", lambda en: en.memset(sruns[:], 0.0), writes=("sruns",))
            for j in range(NCORES):
                dma("sp", wbc[:], bass.AP(wrow_sp.tensor, j * 128, [[0, 128], [1, 128]]),
                    reads=("wrow_sp",), writes=("wbc",))
                dma("sp", sjb[:], sg_ag[j * 128:(j + 1) * 128, :],
                    reads=("sg_ag",), writes=("sjb",))
                for g in range(NG):
                    wvv = bass.AP(wbc.ap().tensor, g * 16,
                                  [[128, 128], [1, 16], [0, 64]])
                    p.op("dve", lambda en, g=g, wvv=wvv: en.tensor_tensor(
                        out=tmpc[:], in0=sjb[:, g * 1024:(g + 1) * 1024], in1=wvv,
                        op=ALU.mult), reads=("sjb", "wbc"), writes=("tmpc",))
                    p.op("dve", lambda en, g=g: en.tensor_tensor(
                        out=sruns[:, g * 1024:(g + 1) * 1024],
                        in0=sruns[:, g * 1024:(g + 1) * 1024], in1=tmpc[:],
                        op=ALU.add), reads=("sruns", "tmpc"), writes=("sruns",))
        _sub.close()
        p.barrier()

        pa_sl = lambda g, h2: tuple(f"pA/{hh}" for hh in range(h2 * 8, h2 * 8 + 8))
        _sub = ExitStack()
        pG = _sub.enter_context(nc.psum_tensor("pG", [128, 128], F32))
        pdf0 = _sub.enter_context(nc.psum_tensor("pdf0", [128, 128], F32))
        pdf1 = _sub.enter_context(nc.psum_tensor("pdf1", [128, 128], F32))
        pAps = _sub.enter_context(nc.psum_tensor("pAps", [128, 1024], F32))
        pBps = _sub.enter_context(nc.psum_tensor("pBps", [128, 1024], F32))
        srunb = _sub.enter_context(nc.sbuf_tensor("srunb", [128, DIN], BF16))
        c_sb2 = _sub.enter_context(nc.sbuf_tensor("c_sb2", [128, NG * 128], BF16))
        bTf_sb = _sub.enter_context(nc.sbuf_tensor("bTf_sb", [128, NG * 128], BF16))
        sz_tok = _sub.enter_context(nc.sbuf_tensor("sz_tok", [128, DIN], BF16))
        diffw = _sub.enter_context(nc.sbuf_tensor("diffw", [2, 16 * 128], F32))
        diffv = _sub.enter_context(nc.sbuf_tensor("diffv", [2, 16 * 128], F32))
        lbuf = _sub.enter_context(nc.sbuf_tensor("lbuf", [128, 2 * 128], F32))
        lexp = _sub.enter_context(nc.sbuf_tensor("lexp", [128, 2 * 128], F32))
        m2b = _sub.enter_context(nc.sbuf_tensor("m2b", [128, 2 * 128], BF16))
        xD = _sub.enter_context(nc.sbuf_tensor("xD", [128, 1024], BF16))
        ygb = _sub.enter_context(nc.sbuf_tensor("ygb", [128, DIN], F32))
        ynb = _sub.enter_context(nc.sbuf_tensor("ynb", [128, 1024], BF16))
        dump8 = _sub.enter_context(nc.sbuf_tensor("dump8", [128, 1024], F32))
        sqacc = _sub.enter_context(nc.sbuf_tensor("sqacc", [128, 16], F32))
        c_nw = _sub.enter_context(nc.sbuf_tensor("c_nw", [128, DIN], BF16))
        if True:
            dma("pool", c_nw[:], bass.AP(norm_w.tensor, 0, [[0, 128], [1, DIN]]),
                writes=("c_nw",))
            p.op("dve", lambda en: en.memset(diffw[0:2, :], 1.0), writes=("diffw_c", "diffw_r"))
            p.op("dve", lambda en: en.memset(diffv[0:2, :], -1.0), writes=("diffv_c", "diffv_r"))
            for c in range(NCH):
                c0 = c * 128
                chunk_prep(c, need_od=True)
                p.op("dve", lambda en: en.tensor_copy(out=srunb[:], in_=sruns[:]),
                     reads=("sruns",), writes=("srunb",))
                dma("sp", bass.AP(c_sb2.ap().tensor, 0,
                                  [[NG * 128, 128], [128, NG], [1, 128]]),
                    bass.AP(xbcT_sp.tensor, (DIN + NG * DS) * T + c0,
                            [[T, 128], [T * 128, NG], [1, 128]]),
                    reads=("xbcT_sp",), writes=("c_sb2",), tag="cld")
                dma("sp", bass.AP(bTf_sb.ap().tensor, 0,
                                  [[NG * 128, 128], [128, NG], [1, 128]]),
                    bass.AP(xbcT_sp.tensor, DIN * T + c0,
                            [[T, 128], [T * 128, NG], [1, 128]]),
                    reads=("xbcT_sp",), writes=("bTf_sb",), tag="bfld")
                dmaT("sp", sz_tok[:], zT_sp[0:DIN, c0:c0 + 128],
                     reads=("zT_sp",), writes=("sz_tok",), tag="szT")
                for g in range(NG):
                    dma("sp", diffw[1:2, :],
                        bass.AP(csf_sp.tensor, g * 16 * T + c0, [[1, 1], [T, 16], [1, 128]]),
                        reads=("csf_sp",), writes=("diffw_r",), tag="csrw")
                    dma("sp", diffv[0:1, :],
                        bass.AP(csf_sp.tensor, g * 16 * T + c0, [[1, 1], [T, 16], [1, 128]]),
                        reads=("csf_sp",), writes=("diffv_r",), tag="csrv")
                    dvv = bass.AP(c_D.ap().tensor, g * 16, [[128, 128], [1, 16], [0, 64]])
                    p.op("dve", lambda en, g=g, dvv=dvv: en.tensor_tensor(
                        out=xD[:],
                        in0=x_tok[:, g * 1024:(g + 1) * 1024], in1=dvv, op=ALU.mult),
                        reads=("x_tok", "c_D"), writes=("xD",))
                    p.op("pe", lambda en, g=g: en.matmul(
                        out=pG[:], lhsT=bTf_sb[:, g * 128:(g + 1) * 128],
                        rhs=c_sb2[:, g * 128:(g + 1) * 128], start=True, stop=True),
                        reads=("bTf_sb", "c_sb2"), writes=("pG",), tag="gmm")
                    for hh in range(16):
                        h = g * 16 + hh
                        par = hh % 2
                        pdf = (pdf0, pdf1)[par]
                        p.op("pe", lambda en, hh=hh, pdf=pdf: en.matmul(
                            out=pdf[:], lhsT=diffw[:, hh * 128:(hh + 1) * 128],
                            rhs=diffv[:, hh * 128:(hh + 1) * 128],
                            start=True, stop=True),
                            reads=("diffw_c", "diffw_r", "diffv_c", "diffv_r"),
                            writes=(f"pdf/{par}",), tag="dfmm")
                        p.op("dve", lambda en, par=par, pdf=pdf: en.select(
                            out=lbuf[:, par * 128:(par + 1) * 128], mask=c_tri[:],
                            on_true=pdf[:], on_false=c_neg[:], add_drain=True),
                            reads=(f"pdf/{par}", "c_tri", "c_neg"),
                            writes=(f"lbuf/{par}",))
                        p.op("act", lambda en, par=par: en.activation(
                            out=lexp[:, par * 128:(par + 1) * 128],
                            in_=lbuf[:, par * 128:(par + 1) * 128], func=AF.Exp),
                            reads=(f"lbuf/{par}",), writes=(f"lexp/{par}",))
                        p.op("dve", lambda en, h=h, par=par: en.scalar_tensor_tensor(
                            out=m2b[:, par * 128:(par + 1) * 128], in0=pG[:],
                            scalar=tT[:, h:h + 1],
                            in1=lexp[:, par * 128:(par + 1) * 128],
                            op0=ALU.mult, op1=ALU.mult),
                            reads=("pG", "tT_dt", f"lexp/{par}"),
                            writes=(f"m2/{par}",))
                        p.op("pe", lambda en, h=h, hh=hh, par=par: en.matmul(
                            out=pAps[:, hh * 64:(hh + 1) * 64],
                            lhsT=m2b[:, par * 128:(par + 1) * 128],
                            rhs=x_tok[:, h * 64:(h + 1) * 64],
                            start=(hh % 8 == 0), stop=False),
                            reads=(f"m2/{par}", "x_tok"),
                            writes=(f"pA/{hh}",), tag="ydmm")
                    for h2 in range(2):
                        p.op("pe", lambda en, g=g, h2=h2: en.matmul(
                            out=pAps[:, h2 * 512:(h2 + 1) * 512],
                            lhsT=c_eye_bf[:],
                            rhs=xD[:, h2 * 512:(h2 + 1) * 512],
                            start=False, stop=True),
                            reads=("c_eye_bf", "xD") + pa_sl(g, h2),
                            writes=pa_sl(g, h2), tag="dxmm")
                        p.op("pe", lambda en, g=g, h2=h2: en.matmul(
                            out=pBps[:, h2 * 512:(h2 + 1) * 512],
                            lhsT=c_sb2[:, g * 128:(g + 1) * 128],
                            rhs=srunb[:, g * 1024 + h2 * 512: g * 1024 + (h2 + 1) * 512],
                            start=True, stop=True),
                            reads=("c_sb2", "srunb"), writes=(f"pB/{h2}",),
                            tag="yomm")
                    odv = bass.AP(tT.ap().tensor, 256 + g * 16,
                                  [[384, 128], [1, 16], [0, 64]])
                    p.op("dve", lambda en, g=g, odv=odv: en.tensor_tensor(
                        out=ygb[:, g * 1024:(g + 1) * 1024], in0=pBps[:], in1=odv,
                        op=ALU.mult),
                        reads=("pB/0", "pB/1", "tT_od"), writes=("ygb",))
                    p.op("dve", lambda en, g=g: en.tensor_tensor(
                        out=ygb[:, g * 1024:(g + 1) * 1024],
                        in0=ygb[:, g * 1024:(g + 1) * 1024], in1=pAps[:], op=ALU.add),
                        reads=("ygb",) + pa_sl(g, 0) + pa_sl(g, 1), writes=("ygb",))
                    p.op("dve", lambda en, g=g: en.tensor_tensor(
                        out=ygb[:, g * 1024:(g + 1) * 1024],
                        in0=ygb[:, g * 1024:(g + 1) * 1024],
                        in1=sz_tok[:, g * 1024:(g + 1) * 1024], op=ALU.mult),
                        reads=("ygb", "sz_tok"), writes=("ygb",))
                    p.op("act", lambda en, g=g: en.activation(
                        out=dump8[:], in_=ygb[:, g * 1024:(g + 1) * 1024],
                        func=AF.Square, accum_out=sqacc[:, g:g + 1]),
                        reads=("ygb",), writes=("dump8", f"sqacc/{g}"))
                p.op("dve", lambda en: en.reduce_sum(
                    out=sqacc[:, 8:9], in_=sqacc[:, 0:8], axis=AX.X),
                    reads=tuple(f"sqacc/{g}" for g in range(NG)), writes=("sqs",))
                p.op("dve", lambda en: en.tensor_scalar(
                    out=sqacc[:, 9:10], in0=sqacc[:, 8:9], scalar1=1.0 / DIN,
                    scalar2=float(EPS), op0=ALU.mult, op1=ALU.add),
                    reads=("sqs",), writes=("ms2",))
                p.op("act", lambda en: en.activation(
                    out=sqacc[:, 10:11], in_=sqacc[:, 9:10], func=AF.Sqrt),
                    reads=("ms2",), writes=("sd2",))
                p.op("dve", lambda en: en.reciprocal(
                    out=sqacc[:, 11:12], in_=sqacc[:, 10:11]),
                    reads=("sd2",), writes=("rstd2",))
                for g in range(NG):
                    p.op("dve", lambda en, g=g: en.scalar_tensor_tensor(
                        out=ynb[:],
                        in0=ygb[:, g * 1024:(g + 1) * 1024],
                        scalar=sqacc[:, 11:12],
                        in1=c_nw[:, g * 1024:(g + 1) * 1024],
                        op0=ALU.mult, op1=ALU.mult),
                        reads=("ygb", "rstd2", "c_nw"), writes=("ynb",))
                    dma("act", yn_sp[c0:c0 + 128, g * 1024:(g + 1) * 1024], ynb[:],
                        reads=("ynb",), writes=("yn_sp",))
                states_and_update(c, pAps, pa_sl)
        _sub.close()
        b_scope.close()
        p.barrier()

        intT_sp = nc.dram_tensor("intT_sp", [INTER, T], F16).ap()
        _sub = ExitStack()
        ynT = _sub.enter_context(nc.sbuf_tensor("ynT", [128, 64 * 512], BF16))
        wos4 = _sub.enter_context(nc.sbuf_tensor("wos4", [128, 2 * 512], BF16))
        xTre = _sub.enter_context(nc.sbuf_tensor("xTre", [128, 2 * 512], F32))
        x1b = _sub.enter_context(nc.sbuf_tensor("x1b", [128, 2 * 512], F32))
        sqb = _sub.enter_context(nc.sbuf_tensor("sqb", [128, 512], F32))
        statc = _sub.enter_context(nc.sbuf_tensor("statc", [1, 8 * 512], F32))
        mbcb = _sub.enter_context(nc.sbuf_tensor("mbcb", [128, 512], F32))
        rbcb = _sub.enter_context(nc.sbuf_tensor("rbcb", [128, 512], F32))
        x2b = _sub.enter_context(nc.sbuf_tensor("x2b", [128, 512], F16))
        pC = _sub.enter_context(nc.psum_tensor("pC", [128, 2048], F32))
        pLNm = _sub.enter_context(nc.psum_tensor("pLNm", [1, 512], F32))
        pLNs = _sub.enter_context(nc.psum_tensor("pLNs", [1, 512], F32))
        pBC = _sub.enter_context(nc.psum_tensor("pBC", [128, 512], F32))
        if True:
            for tcc in range(T // 512):
                tb = tcc * 512
                for fk in range(64):
                    dmaT("sp", ynT[:, fk * 512:(fk + 1) * 512],
                         yn_sp[tb:tb + 512, fk * 128:(fk + 1) * 128],
                         reads=("yn_sp",), writes=("ynT",), tag="ynT")
                nstat = 0
                for dtl4 in range(8):
                    slot = dtl4 % 2
                    d0 = dtl4 * 512
                    for fk in range(64):
                        dmaT("sp", wos4[:, slot * 512:(slot + 1) * 512],
                             w_out_bf[d0:d0 + 512, fk * 128:(fk + 1) * 128],
                             reads=(f"woutcast/{dtl4}",), writes=(f"wos4/{slot}",),
                             tag="wosT")
                        for dsub in range(4):
                            last = fk == 63
                            p.op("pe", lambda en, fk=fk, dsub=dsub, slot=slot: en.matmul(
                                out=pC[:, dsub * 512:(dsub + 1) * 512],
                                lhsT=wos4[:, slot * 512 + dsub * 128:
                                          slot * 512 + (dsub + 1) * 128],
                                rhs=ynT[:, fk * 512:(fk + 1) * 512],
                                start=(fk == 0), stop=(fk == 63)),
                                reads=(f"wos4/{slot}", "ynT"),
                                writes=(f"pC/{dsub}",) if (last or fk == 0) else (),
                                inc=(last or dsub == 3),
                                tag="opmm")
                    for dsub in range(4):
                        d1 = d0 + dsub * 128
                        sx = dsub % 2
                        dma("sp", xTre[:, sx * 512:(sx + 1) * 512],
                            xT_sp[d1:d1 + 128, tb:tb + 512],
                            reads=("xT_sp",), writes=(f"xTre/{sx}",), tag="xre")
                        p.op("dve", lambda en, dsub=dsub, sx=sx: en.tensor_tensor(
                            out=x1b[:, sx * 512:(sx + 1) * 512],
                            in0=pC[:, dsub * 512:(dsub + 1) * 512],
                            in1=xTre[:, sx * 512:(sx + 1) * 512], op=ALU.add),
                            reads=(f"pC/{dsub}", f"xTre/{sx}"), writes=(f"x1b/{sx}",))
                        dma("act", x1T_sp[d1:d1 + 128, tb:tb + 512],
                            x1b[:, sx * 512:(sx + 1) * 512],
                            reads=(f"x1b/{sx}",), writes=("x1T_sp",))
                        p.op("act", lambda en, sx=sx: en.activation(
                            out=sqb[:], in_=x1b[:, sx * 512:(sx + 1) * 512],
                            func=AF.Square),
                            reads=(f"x1b/{sx}",), writes=("sqb",))
                        first = nstat == 0
                        lastst = nstat == 31
                        p.op("pe", lambda en, sx=sx, fi_=first, la_=lastst: en.matmul(
                            out=pLNm[:], lhsT=c_oc[:],
                            rhs=x1b[:, sx * 512:(sx + 1) * 512],
                            start=fi_, stop=la_),
                            reads=(f"x1b/{sx}", "c_oc"),
                            writes=("pLNm",) if (lastst or first) else (), inc=True,
                            tag="lnm")
                        p.op("pe", lambda en, fi_=first, la_=lastst: en.matmul(
                            out=pLNs[:], lhsT=c_oc[:], rhs=sqb[:],
                            start=fi_, stop=la_),
                            reads=("sqb", "c_oc"),
                            writes=("pLNs",) if (lastst or first) else (), inc=True,
                            tag="lns")
                        nstat += 1
                p.op("dve", lambda en: en.tensor_scalar(
                    out=statc[0:1, 0:512], in0=pLNm[:], scalar1=1.0 / DM,
                    scalar2=None, op0=ALU.mult), reads=("pLNm",), writes=("c_m",))
                p.op("dve", lambda en: en.tensor_tensor(
                    out=statc[0:1, 512:1024], in0=statc[0:1, 0:512],
                    in1=statc[0:1, 0:512], op=ALU.mult),
                    reads=("c_m",), writes=("c_msq",))
                p.op("dve", lambda en: en.scalar_tensor_tensor(
                    out=statc[0:1, 1024:1536], in0=pLNs[:], scalar=1.0 / DM,
                    in1=statc[0:1, 512:1024], op0=ALU.mult, op1=ALU.subtract),
                    reads=("pLNs", "c_msq"), writes=("c_var",))
                p.op("act", lambda en: en.activation(
                    out=statc[0:1, 1536:2048], in_=statc[0:1, 1024:1536],
                    func=AF.Sqrt, bias=c_eps[0:1, 0:1]),
                    reads=("c_var", "c_eps"), writes=("c_sd",))
                p.op("dve", lambda en: en.reciprocal(
                    out=statc[0:1, 2048:2560], in_=statc[0:1, 1536:2048]),
                    reads=("c_sd",), writes=("c_rstd",))
                p.op("pe", lambda en: en.matmul(
                    out=pBC[:], lhsT=c_or[:], rhs=statc[0:1, 0:512],
                    start=True, stop=True),
                    reads=("c_or", "c_m"), writes=("pBC",), tag="mbc")
                p.op("dve", lambda en: en.tensor_copy(out=mbcb[:], in_=pBC[:]),
                     reads=("pBC",), writes=("mbcb",))
                p.op("pe", lambda en: en.matmul(
                    out=pBC[:], lhsT=c_or[:], rhs=statc[0:1, 2048:2560],
                    start=True, stop=True),
                    reads=("c_or", "c_rstd", "mbcb"), writes=("pBC",), tag="rbc")
                p.op("dve", lambda en: en.tensor_copy(out=rbcb[:], in_=pBC[:]),
                     reads=("pBC",), writes=("rbcb",))
                for dtl in range(32):
                    d1 = dtl * 128
                    dma("sp", xTre[:, 0:512], x1T_sp[d1:d1 + 128, tb:tb + 512],
                        reads=("x1T_sp",), writes=("xTre/0",), tag="x1re")
                    p.op("dve", lambda en: en.tensor_tensor(
                        out=x1b[:, 0:512], in0=xTre[:, 0:512], in1=mbcb[:],
                        op=ALU.subtract),
                        reads=("xTre/0", "mbcb"), writes=("x1b/0",))
                    p.op("dve", lambda en: en.tensor_tensor(
                        out=x2b[:], in0=x1b[:, 0:512], in1=rbcb[:], op=ALU.mult),
                        reads=("x1b/0", "rbcb"), writes=("x2b",))
                    dma("act", x2T_sp[d1:d1 + 128, tb:tb + 512], x2b[:],
                        reads=("x2b",), writes=("x2T_sp",))
        _sub.close()
        p.barrier()

        zpg_sb = E(nc.sbuf_tensor("zpg_sb", [32, INTER], F16))
        zpu_sb = E(nc.sbuf_tensor("zpu_sb", [32, INTER], F16))
        zpd_sb = E(nc.sbuf_tensor("zpd_sb", [NKD, DM], F16))

        def build_zp(zsrc, ssrc, dst, G, N, rnames):
            _sub = ExitStack()
            zw = _sub.enter_context(nc.sbuf_tensor("zw" + rnames, [G, N // 8], I32))
            zf = _sub.enter_context(nc.sbuf_tensor("zf" + rnames, [G, N], I32))
            zf2 = _sub.enter_context(nc.sbuf_tensor("zg" + rnames, [G, N], F32))
            zs = _sub.enter_context(nc.sbuf_tensor("zs" + rnames, [G, N], F16))
            if True:
                dma("sp", zw[:], zsrc, writes=(rnames + "zw",))
                dma("sp", zs[:], ssrc, writes=(rnames + "zs",))
                for jn in range(8):
                    dstv = bass.AP(zf.ap().tensor, jn, [[N, G], [8, N // 8]])
                    p.op("dve", lambda en, jn=jn, dstv=dstv, zw=zw: en.tensor_scalar(
                        out=dstv, in0=zw[:], scalar1=4 * jn, scalar2=15,
                        op0=ALU.logical_shift_right, op1=ALU.bitwise_and),
                        reads=(rnames + "zw",), writes=(rnames + f"zf{jn}",))
                p.op("dve", lambda en, zf=zf, zf2=zf2: en.tensor_copy(
                    out=zf2[:], in_=zf[:]),
                    reads=tuple(rnames + f"zf{j}" for j in range(8)),
                    writes=(rnames + "zfc",))
                p.op("dve", lambda en, zf2=zf2: en.tensor_scalar(
                    out=zf2[:], in0=zf2[:], scalar1=-1.0, scalar2=-1.0,
                    op0=ALU.mult, op1=ALU.add),
                    reads=(rnames + "zfc",), writes=(rnames + "zfn",))
                p.op("dve", lambda en, zf2=zf2, zs=zs, dst=dst: en.tensor_tensor(
                    out=dst, in0=zf2[:], in1=zs[:], op=ALU.mult),
                    reads=(rnames + "zfn", rnames + "zs"), writes=(rnames + "zp",))

            _sub.close()
        build_zp(g_z, g_s, zpg_sb[:], 32, INTER, "g")
        p.barrier()
        build_zp(u_z, u_s, zpu_sb[:], 32, INTER, "u")
        p.barrier()
        build_zp(d_z, d_s, zpd_sb[:], NKD, DM, "d")
        p.barrier()

        def dequant(qw_src, sr_src, dq_dst, K, N, CC, label):
            NR = cdiv(K // 8, 128)
            _sub = ExitStack()
            qwb = _sub.enter_context(nc.sbuf_tensor("qw" + label, [128, 2 * CC], I32))
            ssb = _sub.enter_context(nc.sbuf_tensor("ss" + label, [128, 2 * CC], F16))
            nib = _sub.enter_context(nc.sbuf_tensor("nb" + label, [128, CC], I16))
            wob = _sub.enter_context(nc.sbuf_tensor("wo" + label, [128, 2 * CC], F16))
            if True:
                for R in range(NR):
                    r0 = R * 128
                    r1 = min(K // 8, r0 + 128)
                    rw = r1 - r0
                    for cc in range(N // CC):
                        c0 = cc * CC
                        sl = (R * (N // CC) + cc) % 2
                        dma("sp", qwb[0:rw, sl * CC:sl * CC + CC],
                            qw_src[r0:r1, c0:c0 + CC],
                            writes=(label + f"qw/{sl}",), tag="dqql")
                        dma("sp", ssb[0:rw, sl * CC:sl * CC + CC],
                            sr_src[r0:r1, c0:c0 + CC],
                            writes=(label + f"ss/{sl}",), tag="dqsl")
                        qv = qwb.ap().bitcast(I16)
                        for j in range(8):
                            half = j // 4
                            sh = 4 * (j % 4)
                            src_ap = qv[0:rw, 2 * sl * CC + half:
                                        2 * sl * CC + half + 2 * CC - 1:2]
                            p.op("dve", lambda en, src_ap=src_ap, sh=sh, rw=rw:
                                 en.tensor_scalar(
                                     out=nib[0:rw, :], in0=src_ap, scalar1=sh,
                                     scalar2=15, op0=ALU.logical_shift_right,
                                     op1=ALU.bitwise_and),
                                 reads=(label + f"qw/{sl}",),
                                 writes=(label + "nib",))
                            p.op("dve", lambda en, sl=sl, rw=rw, j=j: en.tensor_tensor(
                                 out=wob[0:rw, (j % 2) * CC:(j % 2) * CC + CC],
                                 in0=nib[0:rw, :],
                                 in1=ssb[0:rw, sl * CC:sl * CC + CC], op=ALU.mult),
                                 reads=(label + "nib", label + f"ss/{sl}"),
                                 writes=(label + f"wo/{j % 2}",))
                            dstv = bass.AP(dq_dst.tensor,
                                           (r0 * 8 + j) * N + c0,
                                           [[8 * N, rw], [1, CC]])
                            dma("act", dstv, wob[0:rw, (j % 2) * CC:(j % 2) * CC + CC],
                                reads=(label + f"wo/{j % 2}",),
                                writes=(label + "dq",), tag="dqst")

            _sub.close()
        dequant(g_qw, g_sr, wg_dq, DM, INTER, 2752, "g")
        p.barrier()
        dequant(u_qw, u_sr, wu_dq, DM, INTER, 2752, "u")
        p.barrier()
        dequant(d_qw, d_sr, wd_dq, INTER, DM, 2048, "d")
        p.barrier()

        NTH = T // 1024 if T >= 1024 else 1
        THW = T // NTH
        _sub = ExitStack()
        x2p = _sub.enter_context(nc.sbuf_tensor("x2p", [128, 32 * 1024], F16))
        wgsl = _sub.enter_context(nc.sbuf_tensor("wgsl", [128, 2 * 32 * 128], F16))
        wusl = _sub.enter_context(nc.sbuf_tensor("wusl", [128, 2 * 32 * 128], F16))
        xsr = _sub.enter_context(nc.sbuf_tensor("xsr", [1, 512], F32))
        xsg16 = _sub.enter_context(nc.sbuf_tensor("xsg16", [32, 1024], F16))
        sgb = _sub.enter_context(nc.sbuf_tensor("sgb", [128, 512], F32))
        ib = _sub.enter_context(nc.sbuf_tensor("ib", [128, 2 * 512], F16))
        pGa = _sub.enter_context(nc.psum_tensor("pGa", [128, 512], F32))
        pUa = _sub.enter_context(nc.psum_tensor("pUa", [128, 512], F32))
        pGb = _sub.enter_context(nc.psum_tensor("pGb", [128, 512], F32))
        pUb = _sub.enter_context(nc.psum_tensor("pUb", [128, 512], F32))
        pX = _sub.enter_context(nc.psum_tensor("pX", [1, 512], F32))
        if True:
            psG = [pGa, pGb]
            psU = [pUa, pUb]
            for th in range(NTH):
                tb0 = th * THW
                for kt in range(32):
                    dma("sp", x2p[:, kt * THW:(kt + 1) * THW],
                        x2T_sp[kt * 128:(kt + 1) * 128, tb0:tb0 + THW],
                        reads=("x2T_sp",), writes=("x2p",), tag="x2pl")
                for kt in range(32):
                    for tl in range(THW // 512):
                        p.op("pe", lambda en, kt=kt, tl=tl: en.matmul(
                            out=pX[:], lhsT=c_ocf16[:],
                            rhs=x2p[:, kt * THW + tl * 512: kt * THW + (tl + 1) * 512],
                            start=True, stop=True),
                            reads=("x2p", "c_ocf16"), writes=("pX",), tag="xsmm")
                        p.op("dve", lambda en: en.tensor_copy(out=xsr[:], in_=pX[:]),
                             reads=("pX",), writes=("xsr",))
                        dma("act", xsg_sp[kt:kt + 1, tb0 + tl * 512:tb0 + (tl + 1) * 512],
                            xsr[:], reads=("xsr",), writes=("xsg_sp",))
                dma("pool", xsg16[:, 0:THW], xsg_sp[:, tb0:tb0 + THW],
                    reads=("xsg_sp",), writes=("xsg16",))
                for it in range(INTER // 128):
                    slw = it % 2
                    dma("sp", wgsl[:, slw * 4096:(slw + 1) * 4096],
                        bass.AP(wg_dq.tensor, it * 128,
                                [[INTER, 128], [INTER * 128, 32], [1, 128]]),
                        reads=("gdq",), writes=(f"wgsl/{slw}",), tag="wgl")
                    dma("sp", wusl[:, slw * 4096:(slw + 1) * 4096],
                        bass.AP(wu_dq.tensor, it * 128,
                                [[INTER, 128], [INTER * 128, 32], [1, 128]]),
                        reads=("udq",), writes=(f"wusl/{slw}",), tag="wul")
                    for tl in range(THW // 512):
                        tcc = th * (THW // 512) + tl
                        sp_ = (it * (THW // 512) + tl) % 2
                        pg, pu = psG[sp_], psU[sp_]
                        for kt in range(32):
                            p.op("pe", lambda en, kt=kt, tl=tl, slw=slw, pg=pg: en.matmul(
                                out=pg[:],
                                lhsT=wgsl[:, slw * 4096 + kt * 128: slw * 4096 + (kt + 1) * 128],
                                rhs=x2p[:, kt * THW + tl * 512: kt * THW + (tl + 1) * 512],
                                start=(kt == 0), stop=False),
                                reads=(f"wgsl/{slw}", "x2p"),
                                writes=(f"psG/{sp_}",) if kt == 0 else (),
                                inc=(kt == 31), tag="gmm2")
                            p.op("pe", lambda en, kt=kt, tl=tl, slw=slw, pu=pu: en.matmul(
                                out=pu[:],
                                lhsT=wusl[:, slw * 4096 + kt * 128: slw * 4096 + (kt + 1) * 128],
                                rhs=x2p[:, kt * THW + tl * 512: kt * THW + (tl + 1) * 512],
                                start=(kt == 0), stop=False),
                                reads=(f"wusl/{slw}", "x2p"),
                                writes=(f"psU/{sp_}",) if kt == 0 else (),
                                inc=(kt == 31), tag="umm2")
                        p.op("pe", lambda en, it=it, tl=tl, pg=pg: en.matmul(
                            out=pg[:], lhsT=zpg_sb[:, it * 128:(it + 1) * 128],
                            rhs=xsg16[:, tl * 512:(tl + 1) * 512],
                            start=False, stop=True),
                            reads=("gzp", "xsg16"), writes=(f"psG/{sp_}",), tag="gzpmm")
                        p.op("pe", lambda en, it=it, tl=tl, pu=pu: en.matmul(
                            out=pu[:], lhsT=zpu_sb[:, it * 128:(it + 1) * 128],
                            rhs=xsg16[:, tl * 512:(tl + 1) * 512],
                            start=False, stop=True),
                            reads=("uzp", "xsg16"), writes=(f"psU/{sp_}",), tag="uzpmm")
                        p.op("act", lambda en, pg=pg: en.activation(
                            out=sgb[:], in_=pg[:], func=AF.Silu),
                            reads=(f"psG/{sp_}",), writes=("sgb",))
                        p.op("dve", lambda en, sp_=sp_, pu=pu: en.tensor_tensor(
                            out=ib[:, sp_ * 512:(sp_ + 1) * 512], in0=sgb[:], in1=pu[:],
                            op=ALU.mult),
                            reads=("sgb", f"psU/{sp_}"), writes=(f"ib/{sp_}",))
                        dma("act", intT_sp[it * 128:(it + 1) * 128,
                                           tcc * 512:(tcc + 1) * 512],
                            ib[:, sp_ * 512:(sp_ + 1) * 512],
                            reads=(f"ib/{sp_}",), writes=("intT_sp",))
                        p.op("pe", lambda en, sp_=sp_: en.matmul(
                            out=pX[:], lhsT=c_ocf16[:],
                            rhs=ib[:, sp_ * 512:(sp_ + 1) * 512],
                            start=True, stop=True),
                            reads=(f"ib/{sp_}", "c_ocf16"), writes=("pX",), tag="xsdmm")
                        p.op("dve", lambda en: en.tensor_copy(out=xsr[:], in_=pX[:]),
                             reads=("pX",), writes=("xsr",))
                        dma("act", xsd_sp[it:it + 1, tcc * 512:(tcc + 1) * 512], xsr[:],
                            reads=("xsr",), writes=("xsd_sp",))
                p.barrier()
        _sub.close()
        p.barrier()

        _sub = ExitStack()
        isb = _sub.enter_context(nc.sbuf_tensor("isb", [128, NKD * 512], F16))
        wdsl = _sub.enter_context(nc.sbuf_tensor("wdsl", [128, 2 * 2 * 512], F16))
        xsd16 = _sub.enter_context(nc.sbuf_tensor("xsd16", [NKD, T], F16))
        ob16 = _sub.enter_context(nc.sbuf_tensor("ob16", [128, 512], F16))
        x1re = _sub.enter_context(nc.sbuf_tensor("x1re", [128, 2 * 512], F32))
        otb = _sub.enter_context(nc.sbuf_tensor("otb", [128, 2 * 512], F32))
        pD = _sub.enter_context(nc.psum_tensor("pD", [128, 2048], F32))
        if True:
            dma("pool", xsd16[:], xsd_sp, reads=("xsd_sp",), writes=("xsd16",))
            for tcc in range(T // 512):
                tb = tcc * 512
                dma("sp", bass.AP(isb.ap().tensor, 0,
                                  [[NKD * 512, 128], [512, NKD], [1, 512]]),
                    bass.AP(intT_sp.tensor, tb,
                            [[T, 128], [T * 128, NKD], [1, 512]]),
                    reads=("intT_sp",), writes=("isb",), tag="isbl")
                for dc in range(8):
                    d0 = dc * 512
                    for ktg in range(NKD // 2):
                        slk = ktg % 2
                        dma("sp", wdsl[:, slk * 1024:(slk + 1) * 1024],
                            bass.AP(wd_dq.tensor, (2 * ktg * 128) * DM + d0,
                                    [[DM, 128], [DM * 128, 2], [1, 512]]),
                            reads=("ddq",), writes=(f"wdsl/{slk}",), tag="wdl")
                        for k2 in range(2):
                            kt = 2 * ktg + k2
                            for dsub in range(4):
                                p.op("pe", lambda en, slk=slk, k2=k2, kt=kt,
                                     dsub=dsub: en.matmul(
                                         out=pD[:, dsub * 512:(dsub + 1) * 512],
                                         lhsT=wdsl[:, slk * 1024 + k2 * 512 + dsub * 128:
                                                   slk * 1024 + k2 * 512 + (dsub + 1) * 128],
                                         rhs=isb[:, kt * 512:(kt + 1) * 512],
                                         start=(kt == 0), stop=False),
                                     reads=(f"wdsl/{slk}", "isb"),
                                     writes=(f"pD/{dsub}",) if kt == 0 else (),
                                     inc=(dsub == 3), tag="dmm")
                    for dsub in range(4):
                        p.op("pe", lambda en, dc=dc, dsub=dsub, tcc=tcc: en.matmul(
                            out=pD[:, dsub * 512:(dsub + 1) * 512],
                            lhsT=zpd_sb[:, dc * 512 + dsub * 128:
                                        dc * 512 + (dsub + 1) * 128],
                            rhs=xsd16[:, tcc * 512:(tcc + 1) * 512],
                            start=False, stop=True),
                            reads=("dzp", "xsd16"), writes=(f"pD/{dsub}",),
                            tag="dzpmm")
                    for dsub in range(4):
                        d1 = d0 + dsub * 128
                        sx = dsub % 2
                        p.op("dve", lambda en, dsub=dsub: en.tensor_copy(
                            out=ob16[:], in_=pD[:, dsub * 512:(dsub + 1) * 512]),
                            reads=(f"pD/{dsub}",), writes=("ob16",))
                        dma("sp", x1re[:, sx * 512:(sx + 1) * 512],
                            x1T_sp[d1:d1 + 128, tb:tb + 512],
                            reads=("x1T_sp",), writes=(f"x1re/{sx}",), tag="x1rd")
                        p.op("dve", lambda en, sx=sx: en.tensor_tensor(
                            out=otb[:, sx * 512:(sx + 1) * 512], in0=ob16[:],
                            in1=x1re[:, sx * 512:(sx + 1) * 512], op=ALU.add),
                            reads=("ob16", f"x1re/{sx}"), writes=(f"otb/{sx}",))
                        dma("act", outT_sp[d1:d1 + 128, tb:tb + 512],
                            otb[:, sx * 512:(sx + 1) * 512],
                            reads=(f"otb/{sx}",), writes=("outT_sp",))
        _sub.close()
        p.barrier()

        _sub = ExitStack()
        oslab = _sub.enter_context(nc.sbuf_tensor("oslab", [128, 32 * 128], F32))
        fstg = _sub.enter_context(nc.sbuf_tensor("fstg", [128, DM], F32))
        pF = _sub.enter_context(nc.psum_tensor("pF", [128, 128], F32))
        if True:
            for tt in range(T // 128):
                t0 = tt * 128
                dma("sp", bass.AP(oslab.ap().tensor, 0,
                                  [[32 * 128, 128], [128, 32], [1, 128]]),
                    bass.AP(outT_sp.tensor, t0,
                            [[T, 128], [T * 128, 32], [1, 128]]),
                    reads=("outT_sp",), writes=("oslab",), tag="osl")
                for dtl in range(32):
                    p.op("pe", lambda en, dtl=dtl: en.transpose(
                        out=pF[:], in_=oslab[:, dtl * 128:(dtl + 1) * 128],
                        identity=c_eye_f32[:]),
                        reads=("oslab", "c_eye_f32"), writes=("pF",), tag="finT")
                    p.op("dve", lambda en, dtl=dtl: en.tensor_copy(
                        out=fstg[:, dtl * 128:(dtl + 1) * 128], in_=pF[:]),
                        reads=("pF",), writes=("fstg",))
                dma("act", out_t[t0:t0 + 128, :], fstg[:],
                    reads=("fstg",), writes=("out_done",))
        _sub.close()
        p.barrier()
        p.emit(stack)
    return nc


_NC_CACHE = {}


def _constants(core):
    eye = np.eye(128)
    tri = (np.arange(128)[None, :] >= np.arange(128)[:, None]).astype(np.uint8)
    selw = np.zeros((9, 8), np.float32)
    for j in range(8):
        for m in range(8):
            if j < m < core:
                selw[m, j] = 1.0
        if j >= core:
            selw[8, j] = -1e30
    return {
        "eye_bf": eye.astype(bfdt),
        "eye_f32": eye.astype(np.float32),
        "tri": tri,
        "ones_row": np.ones((1, 128), np.float32),
        "selw": selw,
    }


def prepare_in_maps(L0_attn_hidden_states, L0_attn_in_proj_w, L0_attn_conv_w,
                    L0_attn_conv_b, L0_attn_dt_bias, L0_attn_A_log, L0_attn_D,
                    L0_attn_norm_w, L0_attn_out_proj_w, L0_ffn_gate_qweight,
                    L0_ffn_gate_scales, L0_ffn_gate_zeros, L0_ffn_up_qweight,
                    L0_ffn_up_scales, L0_ffn_up_zeros, L0_ffn_down_qweight,
                    L0_ffn_down_scales, L0_ffn_down_zeros):
    x = np.ascontiguousarray(np.asarray(L0_attn_hidden_states, np.float32)[0])
    S = x.shape[0]
    T = S // NCORES
    shared = {
        "in_proj_w": np.asarray(L0_attn_in_proj_w, np.float32),
        "conv_w": np.asarray(L0_attn_conv_w, np.float32),
        "conv_b": np.asarray(L0_attn_conv_b, np.float32).reshape(CONVD, 1),
        "dt_bias": np.asarray(L0_attn_dt_bias, np.float32).reshape(NH, 1),
        "A_log": np.asarray(L0_attn_A_log, np.float32).reshape(NH, 1),
        "D": np.asarray(L0_attn_D, np.float32).reshape(1, NH),
        "norm_w": np.asarray(L0_attn_norm_w, np.float32).reshape(1, DIN),
        "out_proj_w": np.asarray(L0_attn_out_proj_w, np.float32),
        "gate_qweight": np.asarray(L0_ffn_gate_qweight, np.int32),
        "gate_scales": np.asarray(L0_ffn_gate_scales, np.float16),
        "gate_scales_rep": np.repeat(np.asarray(L0_ffn_gate_scales, np.float16),
                                     16, axis=0),
        "gate_zeros": np.asarray(L0_ffn_gate_zeros, np.int32),
        "up_qweight": np.asarray(L0_ffn_up_qweight, np.int32),
        "up_scales": np.asarray(L0_ffn_up_scales, np.float16),
        "up_scales_rep": np.repeat(np.asarray(L0_ffn_up_scales, np.float16),
                                   16, axis=0),
        "up_zeros": np.asarray(L0_ffn_up_zeros, np.int32),
        "down_qweight": np.asarray(L0_ffn_down_qweight, np.int32),
        "down_scales": np.asarray(L0_ffn_down_scales, np.float16),
        "down_scales_rep": np.repeat(np.asarray(L0_ffn_down_scales, np.float16),
                                     16, axis=0),
        "down_zeros": np.asarray(L0_ffn_down_zeros, np.int32),
    }
    in_maps = []
    for c in range(NCORES):
        t0 = c * T
        if c == 0:
            xh = np.concatenate([np.zeros((3, DM), np.float32), x[0:T]], axis=0)
        else:
            xh = x[t0 - 3: t0 + T]
        m = dict(shared)
        m["x"] = np.ascontiguousarray(xh)
        m.update(_constants(c))
        in_maps.append(m)
    return in_maps, T, S


def kernel(**inputs):
    in_maps, T, S = prepare_in_maps(**inputs)
    if T not in _NC_CACHE:
        _NC_CACHE[T] = build_program(T)
    nc = _NC_CACHE[T]
    res = run_bass_kernel_spmd(nc, in_maps, core_ids=list(range(NCORES)))
    out = np.concatenate([r["out"] for r in res.results], axis=0)
    return out.reshape(1, S, DM).astype(np.float32)
